# version 01 (final)
# baseline (speedup 1.0000x reference)
import numpy as np
import ml_dtypes
import concourse.bass as bass
import concourse.mybir as mybir
from concourse import bass_isa as _bass_isa
_ReduceOp = _bass_isa.ReduceOp

F32 = mybir.dt.float32
BF16 = mybir.dt.bfloat16
AF = mybir.ActivationFunctionType
ALU = mybir.AluOpType
AX = mybir.AxisListType

S, D, NT, HD, NH, NKV, F, P = 2048, 2048, 16, 64, 32, 8, 8192, 128
C1 = float(2.0 ** 23)
C2 = float(2.0 ** 52)
PI_LO = float(np.nextafter(np.float32(np.pi), np.float32(0.0)))
TWO_PI = float(2.0 * np.pi)
INV_2PI = float(1.0 / (2.0 * np.pi))
PI_HALF = float(np.pi / 2.0)


def host_constants():
    invf64 = 1.0 / (10000.0 ** (np.arange(0, HD, 2, dtype=np.float64) / HD))
    invf_hi = invf64.astype(np.float32)
    invf_lo = (invf64 - invf_hi.astype(np.float64)).astype(np.float32)
    return {
        "c_ident": np.eye(P, dtype=np.float32),
        "c_ones_bf": np.ones((P, 1), dtype=ml_dtypes.bfloat16),
        "c_ones_f": np.ones((P, 1), dtype=np.float32),
        "c_pos": np.arange(S, dtype=np.float32).reshape(S, 1),
        "c_invf_hi": np.ascontiguousarray(np.broadcast_to(invf_hi, (P, 32))),
        "c_invf_lo": np.ascontiguousarray(np.broadcast_to(invf_lo, (P, 32))),
        "c_ones_row": np.ones((1, P), dtype=np.float32),
    }


def build(nc, dbg=False, upto=99, p1_chunks=None):
    kind_scr = "ExternalOutput" if dbg else "Internal"

    x_in = nc.dram_tensor("x", [S, D], F32, kind="ExternalInput").ap()
    xT_in = nc.dram_tensor("xT", [D, S], F32, kind="ExternalInput").ap()
    wq_in = nc.dram_tensor("Wq", [D, 2048], F32, kind="ExternalInput").ap()
    wk_in = nc.dram_tensor("Wk", [D, 512], F32, kind="ExternalInput").ap()
    wv_in = nc.dram_tensor("Wv", [D, 512], F32, kind="ExternalInput").ap()
    wphi_in = nc.dram_tensor("Wphi", [D, 2048], F32, kind="ExternalInput").ap()
    wo_in = nc.dram_tensor("Wo", [D, 2048], F32, kind="ExternalInput").ap()
    wgu_in = nc.dram_tensor("w_gu", [2 * F, D], F32, kind="ExternalInput").ap()
    wd_in = nc.dram_tensor("w_dn", [D, F], F32, kind="ExternalInput").ap()
    cons = {n: nc.dram_tensor(n, list(v.shape), mybir.dt.from_np(v.dtype), kind="ExternalInput").ap()
            for n, v in host_constants().items()}

    out_d = nc.dram_tensor("out", [S, D], F32, kind="ExternalOutput").ap()
    qkh_d = nc.dram_tensor("s_qkh", [S, 2048], BF16, kind=kind_scr).ap()
    qkl_d = nc.dram_tensor("s_qkl", [S, 2048], BF16, kind=kind_scr).ap()
    kkh_d = nc.dram_tensor("s_kkh", [S, 512], BF16, kind=kind_scr).ap()
    kkl_d = nc.dram_tensor("s_kkl", [S, 512], BF16, kind=kind_scr).ap()
    vh_d = nc.dram_tensor("s_vh", [S, 512], BF16, kind=kind_scr).ap()
    vl_d = nc.dram_tensor("s_vl", [S, 512], BF16, kind=kind_scr).ap()
    xphiT_d = nc.dram_tensor("s_xphiT", [NT, P, S], F32, kind=kind_scr).ap()
    x1_d = nc.dram_tensor("s_x1", [S, D], F32, kind=kind_scr).ap()
    xq1_d = nc.dram_tensor("s_xq1", [S, D], BF16, kind=kind_scr).ap()
    h_d = nc.dram_tensor("s_h", [S, F], F32, kind=kind_scr).ap()
    wdqT_d = nc.dram_tensor("s_wdqT", [F, D], BF16, kind=kind_scr).ap()
    qg_d = nc.dram_tensor("s_qgd", [32, 64], F32, kind="Internal").ap()

    ident = nc.alloc_sbuf_tensor("g_ident", [P, P], F32)
    ones_bf = nc.alloc_sbuf_tensor("g_ones_bf", [P, 1], BF16)
    ones_f = nc.alloc_sbuf_tensor("g_ones_f", [P, 1], F32)
    pos_sb = nc.alloc_sbuf_tensor("g_pos", [P, NT], F32)
    invf_hi = nc.alloc_sbuf_tensor("g_invf_hi", [P, 32], F32)
    invf_lo = nc.alloc_sbuf_tensor("g_invf_lo", [P, 32], F32)
    qsel = nc.alloc_sbuf_tensor("g_qsel", [P, 1], F32)
    cosT = nc.alloc_sbuf_tensor("g_cosT", [P, NT * HD], F32)
    sinT = nc.alloc_sbuf_tensor("g_sinT", [P, NT * HD], F32)
    sinv_sb = nc.alloc_sbuf_tensor("g_sinv", [P, NT], F32)
    hmax_sb = nc.alloc_sbuf_tensor("g_hmax", [P, NT], F32)
    sw_sb = nc.alloc_sbuf_tensor("g_sw", [P, 8], F32)
    scl_sb = nc.alloc_sbuf_tensor("g_scl", [P, 2 * NT], F32)
    wsum_sb = nc.alloc_sbuf_tensor("g_wsum", [P, 64], F32)
    junk = nc.alloc_sbuf_tensor("g_junk", [P, 8], F32)
    ones_row = nc.alloc_sbuf_tensor("g_ones_row", [1, P], F32)

    pbank = [nc.alloc_psum_tensor(f"pb{i}", [P, 512], F32) for i in range(8)]

    def finish():
        s_f = nc.alloc_semaphore("fin")
        with nc.Block() as b:
            @b.sync
            def _(sync):
                for r in range(NT):
                    sync.dma_start(out_d[r * P:(r + 1) * P, 0:P], ident[:]).then_inc(s_f, 16)
                sync.wait_ge(s_f, 16 * NT)

    s0 = nc.alloc_semaphore("p0_ld")
    with nc.Block() as b:
        @b.sync
        def _(sync):
            n = 0
            for name, sb in [("c_ident", ident), ("c_ones_bf", ones_bf), ("c_ones_f", ones_f),
                             ("c_invf_hi", invf_hi), ("c_invf_lo", invf_lo),
                             ("c_ones_row", ones_row)]:
                sync.dma_start(sb[:], cons[name]).then_inc(s0, 16)
                n += 1
            with nc.allow_non_contiguous_dma(reason="tiny pos table"):
                sync.dma_start(pos_sb[:], cons["c_pos"].rearrange("(t p) o -> p (t o)", p=P)).then_inc(s0, 16)
            n += 1
            sync.wait_ge(s0, 16 * n)

    with (
        nc.sbuf_tensor("p0_fr", [P, 32], F32) as fr,
        nc.sbuf_tensor("p0_fh", [P, 32], F32) as fh,
        nc.sbuf_tensor("p0_kk", [P, 32], F32) as kk,
        nc.sbuf_tensor("p0_k64", [P, 32], F32) as k64,
        nc.sbuf_tensor("p0_dd", [P, 32], F32) as ddp,
    ):
        with nc.Block() as b:
            @b.vector
            def _(v):
                v.memset(sw_sb[:, 6:7], 1e-05)
                v.tensor_scalar(qsel[:], ones_f[:], C2, None, ALU.add)
                v.drain()
                v.tensor_scalar(qsel[:], qsel[:], -C2, None, ALU.add)
                for t in range(NT):
                    for phase, ktab in ((0, sinT), (1, cosT)):
                        pcol = pos_sb[:, t:t + 1]
                        v.tensor_scalar(fh[:], invf_hi[:], pcol, None, ALU.mult)
                        v.drain()
                        v.scalar_tensor_tensor(fr[:], invf_lo[:], pcol, fh[:], ALU.mult, ALU.add)
                        if phase == 1:
                            v.drain()
                            v.tensor_scalar(fr[:], fr[:], PI_HALF, None, ALU.add)
                        v.drain()
                        v.tensor_scalar(kk[:], fr[:], INV_2PI, C1, ALU.mult, ALU.add)
                        v.drain()
                        v.tensor_scalar(kk[:], kk[:], -C1, None, ALU.add)
                        v.tensor_scalar(k64[:], fr[:], INV_2PI, C2, ALU.mult, ALU.add)
                        v.drain()
                        v.tensor_scalar(k64[:], k64[:], -C2, None, ALU.add)
                        v.drain()
                        v.tensor_sub(ddp[:], k64[:], kk[:])
                        v.drain()
                        v.scalar_tensor_tensor(kk[:], ddp[:], qsel[:], kk[:], ALU.mult, ALU.add)
                        v.drain()
                        v.scalar_tensor_tensor(fr[:], kk[:], -TWO_PI, fr[:], ALU.mult, ALU.add)
                        v.drain()
                        v.tensor_scalar(fr[:], fr[:], -PI_LO, PI_LO, ALU.max, ALU.min)
                        v.drain()
                        v.tensor_copy(ktab[:, t * HD:t * HD + 32], fr[:])
                        v.drain()
                        v.tensor_copy(ktab[:, t * HD + 32:t * HD + 64], fr[:])
                        v.drain()

        with nc.Block() as b:
            @b.scalar
            def _(sc):
                sc.activation(sinT[:], sinT[:], AF.Sin, bias=0.0, scale=1.0)
                sc.activation(cosT[:], cosT[:], AF.Sin, bias=0.0, scale=1.0)

        with nc.Block() as b:
            @b.vector
            def _(v):
                s4 = sinT[:].rearrange("p (t half c) -> p t half c", t=NT, half=2)
                v.tensor_scalar(s4[:, :, 0:1, :], s4[:, :, 0:1, :], -1.0, None, ALU.mult)

    if upto < 1:
        finish()
        return

    sched = []
    for hs in range(2):
        sched += [("q", i, wq_in, i * 512, hs) for i in range(4)]
        sched += [("k", 0, wk_in, 0, hs), ("v", 0, wv_in, 0, hs)]
        sched += [("phi", i, wphi_in, i * 512, hs) for i in range(4)]
    if p1_chunks is not None:
        sched = sched[:p1_chunks]
    NCH = len(sched)
    UPC = 8

    units = []
    nqk = nv = ntok = nphi = nqu = 0
    for cg, (kind, ci, win, wcol, hs) in enumerate(sched):
        for idx in range(UPC):
            d = dict(u=cg * UPC + idx, cg=cg, kind=kind, ci=ci, hs=hs, idx=idx)
            if kind in ("q", "k"):
                d["iqk"] = nqk; nqk += 1
                d["ut"] = ntok; ntok += 1
                if kind == "q":
                    d["iqg"] = nqu; nqu += 1
            elif kind == "v":
                d["iv"] = nv; nv += 1
                d["ut"] = ntok; ntok += 1
            else:
                d["iphi"] = nphi; nphi += 1
            units.append(d)
    tok_units = [d for d in units if "ut" in d]
    qg_cum = []
    c = 0
    for d in tok_units:
        if "iqg" in d:
            c += 1
        qg_cum.append(c)
    qg_req = [0 if ut < 4 else qg_cum[ut - 4] for ut in range(len(tok_units))]

    s_xld = nc.alloc_semaphore("p1_xld")
    s_xsp = nc.alloc_semaphore("p1_xsp")
    s_wld = nc.alloc_semaphore("p1_wld")
    s_wsp = nc.alloc_semaphore("p1_wsp")
    s_pmm = nc.alloc_semaphore("p1_pmm")
    s_pfree = [nc.alloc_semaphore(f"p1_pf{i}") for i in range(4)]
    s_mn = nc.alloc_semaphore("p1_mn")
    s_exp = nc.alloc_semaphore("p1_exp")
    s_qk = nc.alloc_semaphore("p1_qk")
    s_qksp = nc.alloc_semaphore("p1_qksp")
    s_vh = nc.alloc_semaphore("p1_vh")
    s_vsp = nc.alloc_semaphore("p1_vsp")
    s_qst = nc.alloc_semaphore("p1_qst")
    s_qgm = nc.alloc_semaphore("p1_qgm")
    s_phiev = nc.alloc_semaphore("p1_phiev")
    s_phist = nc.alloc_semaphore("p1_phist")

    from contextlib import ExitStack
    _p1 = ExitStack()
    xTh = _p1.enter_context(nc.sbuf_tensor("p1_xTh", [P, NT * 1024], BF16))
    xTl = _p1.enter_context(nc.sbuf_tensor("p1_xTl", [P, NT * 1024], BF16))
    xTf0 = _p1.enter_context(nc.sbuf_tensor("p1_xTf0", [P, NT * 128], F32))
    xTf1 = _p1.enter_context(nc.sbuf_tensor("p1_xTf1", [P, NT * 128], F32))
    wf0 = _p1.enter_context(nc.sbuf_tensor("p1_wf0", [P, 2 * 512], F32))
    wf1 = _p1.enter_context(nc.sbuf_tensor("p1_wf1", [P, 2 * 512], F32))
    wh0 = _p1.enter_context(nc.sbuf_tensor("p1_wh0", [P, NT * 512], BF16))
    wh1 = _p1.enter_context(nc.sbuf_tensor("p1_wh1", [P, NT * 512], BF16))
    wl0 = _p1.enter_context(nc.sbuf_tensor("p1_wl0", [P, NT * 512], BF16))
    wl1 = _p1.enter_context(nc.sbuf_tensor("p1_wl1", [P, NT * 512], BF16))
    ro_t1 = _p1.enter_context(nc.sbuf_tensor("p1_t1", [P, 512], F32))
    ro_t2 = _p1.enter_context(nc.sbuf_tensor("p1_t2", [P, 512], F32))
    ro_qro = _p1.enter_context(nc.sbuf_tensor("p1_qro", [P, 512], F32))
    ro_mn0 = _p1.enter_context(nc.sbuf_tensor("p1_mn0", [P, 512], F32))
    ro_mn1 = _p1.enter_context(nc.sbuf_tensor("p1_mn1", [P, 512], F32))
    ro_e0 = _p1.enter_context(nc.sbuf_tensor("p1_e0", [P, 512], F32))
    ro_e1 = _p1.enter_context(nc.sbuf_tensor("p1_e1", [P, 512], F32))
    ro_lin = _p1.enter_context(nc.sbuf_tensor("p1_lin", [P, 512], F32))
    qk0 = _p1.enter_context(nc.sbuf_tensor("p1_qk0", [P, 512], F32))
    qk1 = _p1.enter_context(nc.sbuf_tensor("p1_qk1", [P, 512], F32))
    sph0 = _p1.enter_context(nc.sbuf_tensor("p1_sph0", [P, 512], BF16))
    sph1 = _p1.enter_context(nc.sbuf_tensor("p1_sph1", [P, 512], BF16))
    sph2 = _p1.enter_context(nc.sbuf_tensor("p1_sph2", [P, 512], BF16))
    sph3 = _p1.enter_context(nc.sbuf_tensor("p1_sph3", [P, 512], BF16))
    spl0 = _p1.enter_context(nc.sbuf_tensor("p1_spl0", [P, 512], BF16))
    spl1 = _p1.enter_context(nc.sbuf_tensor("p1_spl1", [P, 512], BF16))
    spl2 = _p1.enter_context(nc.sbuf_tensor("p1_spl2", [P, 512], BF16))
    spl3 = _p1.enter_context(nc.sbuf_tensor("p1_spl3", [P, 512], BF16))
    ph0 = _p1.enter_context(nc.sbuf_tensor("p1_ph0", [P, 512], F32))
    ph1 = _p1.enter_context(nc.sbuf_tensor("p1_ph1", [P, 512], F32))
    ph2 = _p1.enter_context(nc.sbuf_tensor("p1_ph2", [P, 512], F32))
    ph3 = _p1.enter_context(nc.sbuf_tensor("p1_ph3", [P, 512], F32))
    if True:
        wf = [wf0, wf1]
        wh = [wh0, wh1]
        wl = [wl0, wl1]
        xTf = [xTf0, xTf1]
        ro_mn = [ro_mn0, ro_mn1]
        ro_e = [ro_e0, ro_e1]
        qk_sb = [qk0, qk1]
        sp_h = [sph0, sph1, sph2, sph3]
        sp_l = [spl0, spl1, spl2, spl3]
        phist = [ph0, ph1, ph2, ph3]

        def spill_dst(kind, ci, hs, st):
            row0 = hs * 1024 + st * P
            if kind == "q":
                return (qkh_d[row0:row0 + P, ci * 512:(ci + 1) * 512],
                        qkl_d[row0:row0 + P, ci * 512:(ci + 1) * 512])
            if kind == "k":
                return (kkh_d[row0:row0 + P, :], kkl_d[row0:row0 + P, :])
            return (vh_d[row0:row0 + P, :], vl_d[row0:row0 + P, :])

        qg_count = {}

        def emit_qg(t, d):
            ci, iqk = d["ci"], d["iqk"]
            slot = d["ut"] % 4
            t.wait_ge(s_qksp, iqk + 1)
            n = qg_count.get(ci, 0)
            bankap = pbank[4 + ci][0:1, 0:512]
            t.matmul(bankap, ones_bf[:], sp_h[slot][:], start=(n == 0), stop=False)
            mm = t.matmul(bankap, ones_bf[:], sp_l[slot][:], start=False, stop=(n == 15))
            mm.then_inc(s_qgm, 1)
            qg_count[ci] = n + 1

        with nc.Block() as b:
            @b.sync
            def _(sync):
                halves = sorted({hh for (_, _, _, _, hh) in sched})
                for hs in halves:
                    for qd in range(8):
                        j = hs * 8 + qd
                        if j >= 2:
                            sync.wait_ge(s_xsp, j - 1)
                        sync.dma_start(
                            xTf[j % 2][:].rearrange("p (a c) -> p a c", a=NT),
                            xT_in.rearrange("(a p) s -> p a s", p=P)[
                                :, :, hs * 1024 + qd * 128: hs * 1024 + (qd + 1) * 128]
                        ).then_inc(s_xld, 16)
                    for cg, (kind, ci, win, wcol, hh) in enumerate(sched):
                        if hh != hs:
                            continue
                        for subl in range(8):
                            j = cg * 8 + subl
                            if j >= 2:
                                sync.wait_ge(s_wsp, j - 1)
                            sync.dma_start(
                                wf[j % 2][:].rearrange("p (a c) -> p a c", a=2),
                                win.rearrange("(a p) c -> p a c", p=P)[
                                    :, subl * 2:(subl + 1) * 2, wcol:wcol + 512]
                            ).then_inc(s_wld, 16)

            @b.vector
            def _(v):
                def split_x(hs):
                    for qd in range(8):
                        j = hs * 8 + qd
                        v.wait_ge(s_xld, 16 * (j + 1))
                        src3 = xTf[j % 2][:].rearrange("p (a c) -> p a c", a=NT)
                        dsth = xTh[:].rearrange("p (a c) -> p a c", a=NT)[:, :, qd * 128:(qd + 1) * 128]
                        dstl = xTl[:].rearrange("p (a c) -> p a c", a=NT)[:, :, qd * 128:(qd + 1) * 128]
                        v.tensor_copy(dsth, src3)
                        v.drain()
                        v.tensor_tensor(dstl, src3, dsth, ALU.subtract).then_inc(s_xsp, 1)
                        v.drain()

                def split_w(cg):
                    for subl in range(8):
                        j = cg * 8 + subl
                        v.wait_ge(s_wld, 16 * (j + 1))
                        if cg >= 2 and subl == 0:
                            v.wait_ge(s_pmm, UPC * (cg - 1))
                        src = wf[j % 2][:]
                        v.tensor_copy(wh[cg % 2][:, subl * 1024:(subl + 1) * 1024], src)
                        v.drain()
                        v.tensor_tensor(wl[cg % 2][:, subl * 1024:(subl + 1) * 1024], src,
                                        wh[cg % 2][:, subl * 1024:(subl + 1) * 1024],
                                        ALU.subtract).then_inc(s_wsp, 1)
                        v.drain()

                def unit_work(d):
                    u, kind, idx = d["u"], d["kind"], d["idx"]
                    bank = u % 4
                    ps = pbank[bank][:, 0:512]
                    if kind in ("q", "k"):
                        iqk = d["iqk"]
                        st_glob = d["hs"] * 8 + idx
                        v.wait_ge(s_pmm, u + 1)
                        ps3 = ps.rearrange("p (h c) -> p h c", h=8)
                        cosA = cosT[:, st_glob * HD:(st_glob + 1) * HD].rearrange(
                            "p (o c) -> p o c", o=1).broadcast_to([P, 8, HD])
                        sinF = sinT[:, st_glob * HD:st_glob * HD + 32].rearrange(
                            "p (o c) -> p o c", o=1).broadcast_to([P, 8, 32])
                        sinS = sinT[:, st_glob * HD + 32:(st_glob + 1) * HD].rearrange(
                            "p (o c) -> p o c", o=1).broadcast_to([P, 8, 32])
                        t13 = ro_t1[:].rearrange("p (h c) -> p h c", h=8)
                        t23 = ro_t2[:].rearrange("p (h c) -> p h c", h=8)
                        v.tensor_tensor(t13, ps3, cosA, ALU.mult)
                        v.tensor_tensor(t23[:, :, 0:32], ps3[:, :, 32:64], sinF, ALU.mult)
                        v.tensor_tensor(t23[:, :, 32:64], ps3[:, :, 0:32], sinS,
                                        ALU.mult).then_inc(s_pfree[bank], 1)
                        v.drain()
                        v.tensor_add(ro_qro[:], ro_t1[:], ro_t2[:])
                        v.drain()
                        if iqk >= 2:
                            v.wait_ge(s_exp, iqk - 1)
                        v.tensor_scalar(ro_mn[iqk % 2][:], ro_qro[:], 0.0, None,
                                        ALU.min).then_inc(s_mn, 1)
                        v.tensor_scalar(ro_lin[:], ro_qro[:], 1.0, None, ALU.add)
                        v.wait_ge(s_exp, iqk + 1)
                        if iqk >= 2:
                            v.wait_ge(s_qksp, iqk - 1)
                        v.drain()
                        v.tensor_max(qk_sb[iqk % 2][:], ro_lin[:],
                                     ro_e[iqk % 2][:]).then_inc(s_qk, 1)
                        v.drain()
                    elif kind == "v":
                        iv, ut = d["iv"], d["ut"]
                        slot = ut % 4
                        v.wait_ge(s_vh, iv + 1)
                        v.tensor_tensor(sp_l[slot][:], ps, sp_h[slot][:],
                                        ALU.subtract).then_inc(s_pfree[bank], 1)
                        v.drain()
                        v.tensor_copy(junk[:, 0:1], ones_f[:]).then_inc(s_vsp, 1)
                        v.drain()

                split_x(0)
                split_w(0)
                if NCH > 1:
                    split_w(1)
                for cg in range(NCH):
                    if cg == 10:
                        v.wait_ge(s_pmm, UPC * 10)
                        split_x(1)
                        split_w(10)
                        if NCH > 11:
                            split_w(11)
                    for d in units[cg * UPC:(cg + 1) * UPC]:
                        unit_work(d)
                    nxt = cg + 2
                    if nxt < NCH and nxt not in (10, 11):
                        split_w(nxt)

            @b.tensor
            def _(t):
                qg_pend = []
                for cg, (kind, ci, win, wcol, hs) in enumerate(sched):
                    t.wait_ge(s_xsp, 8 * (hs + 1))
                    t.wait_ge(s_wsp, 8 * (cg + 1))
                    for d in units[cg * UPC:(cg + 1) * UPC]:
                        u, idx = d["u"], d["idx"]
                        bank = u % 4
                        ps = pbank[bank][:, 0:512]
                        if u >= 4:
                            t.wait_ge(s_pfree[bank], u // 4)
                        mm = None
                        if kind in ("q", "k", "v"):
                            lo = idx * P
                            for pas in range(3):
                                la, rb = ((xTh, wh[cg % 2]), (xTh, wl[cg % 2]), (xTl, wh[cg % 2]))[pas]
                                for kc in range(NT):
                                    mm = t.matmul(ps, la[:, kc * 1024 + lo: kc * 1024 + lo + P],
                                                  rb[:, kc * 512:(kc + 1) * 512],
                                                  start=(pas == 0 and kc == 0),
                                                  stop=(pas == 2 and kc == NT - 1))
                        else:
                            m, sb2 = divmod(idx, 2)
                            for pas in range(3):
                                la, rb = ((wh[cg % 2], xTh), (wh[cg % 2], xTl), (wl[cg % 2], xTh))[pas]
                                for kc in range(NT):
                                    mm = t.matmul(ps, la[:, kc * 512 + m * P: kc * 512 + m * P + P],
                                                  rb[:, kc * 1024 + sb2 * 512: kc * 1024 + (sb2 + 1) * 512],
                                                  start=(pas == 0 and kc == 0),
                                                  stop=(pas == 2 and kc == NT - 1))
                        mm.then_inc(s_pmm, 1)
                        if "iqg" in d:
                            qg_pend.append(d)
                        if kind in ("q", "k"):
                            while qg_pend and qg_pend[0]["iqk"] <= d["iqk"] - 2:
                                emit_qg(t, qg_pend.pop(0))
                for dd2 in qg_pend:
                    emit_qg(t, dd2)

            @b.scalar
            def _(sc):
                for d in units:
                    kind = d["kind"]
                    if kind in ("q", "k"):
                        iqk = d["iqk"]
                        sc.wait_ge(s_mn, iqk + 1)
                        if iqk >= 2:
                            sc.wait_ge(s_qk, iqk - 1)
                        sc.activation(ro_e[iqk % 2][:], ro_mn[iqk % 2][:], AF.Exp,
                                      bias=0.0, scale=1.0).then_inc(s_exp, 1)
                    elif kind == "v":
                        u, ut = d["u"], d["ut"]
                        slot = ut % 4
                        sc.wait_ge(s_pmm, u + 1)
                        if ut >= 4:
                            sc.wait_ge(s_qst, 32 * (ut - 3))
                            sc.wait_ge(s_qgm, qg_req[ut])
                        sc.activation(sp_h[slot][:], pbank[u % 4][:, 0:512], AF.Identity,
                                      bias=0.0, scale=1.0).then_inc(s_vh, 1)
                    else:
                        u, iphi = d["u"], d["iphi"]
                        bank = u % 4
                        slot = iphi % 4
                        sc.wait_ge(s_pmm, u + 1)
                        if iphi >= 4:
                            sc.wait_ge(s_phist, 16 * (iphi - 3))
                        sc.activation(phist[slot][:], pbank[bank][:, 0:512], AF.Identity,
                                      bias=0.0, scale=1.0).then_inc(s_pfree[bank], 1)
                        sc.drain()
                        sc.activation(junk[:, 1:2], ones_f[:], AF.Identity,
                                      bias=0.0, scale=1.0).then_inc(s_phiev, 1)

            @b.gpsimd
            def _(g):
                for d in units:
                    kind = d["kind"]
                    if kind in ("q", "k"):
                        iqk, ut, idx, hs = d["iqk"], d["ut"], d["idx"], d["hs"]
                        slot = ut % 4
                        g.wait_ge(s_qk, iqk + 1)
                        if ut >= 4:
                            g.wait_ge(s_qst, 32 * (ut - 3))
                            g.wait_ge(s_qgm, qg_req[ut])
                        src = qk_sb[iqk % 2]
                        g.tensor_copy(sp_h[slot][:], src[:])
                        g.drain()
                        g.tensor_tensor(sp_l[slot][:], src[:], sp_h[slot][:],
                                        ALU.subtract).then_inc(s_qksp, 1)
                        g.drain()
                        dh, dl = spill_dst(kind, d["ci"], hs, idx)
                        g.dma_start(dh, sp_h[slot][:]).then_inc(s_qst, 16)
                        g.dma_start(dl, sp_l[slot][:]).then_inc(s_qst, 16)
                    elif kind == "v":
                        iv, ut, idx, hs = d["iv"], d["ut"], d["idx"], d["hs"]
                        slot = ut % 4
                        g.wait_ge(s_vsp, iv + 1)
                        dh, dl = spill_dst("v", 0, hs, idx)
                        g.dma_start(dh, sp_h[slot][:]).then_inc(s_qst, 16)
                        g.dma_start(dl, sp_l[slot][:]).then_inc(s_qst, 16)
                    else:
                        iphi, idx, hs = d["iphi"], d["idx"], d["hs"]
                        slot = iphi % 4
                        ci = d["ci"]
                        ft = ci * 4 + idx // 2
                        sb2 = idx % 2
                        g.wait_ge(s_phiev, iphi + 1)
                        g.dma_start(
                            xphiT_d[ft, :, hs * 1024 + sb2 * 512: hs * 1024 + (sb2 + 1) * 512],
                            phist[slot][:]).then_inc(s_phist, 16)
                g.wait_ge(s_qst, 32 * ntok)
                g.wait_ge(s_phist, 16 * nphi)

    _p1.close()

    if upto < 2:
        finish()
        return

    o_acc0 = nc.alloc_sbuf_tensor("g_oacc0", [P, 1024], F32)
    o_acc1 = nc.alloc_sbuf_tensor("g_oacc1", [P, 1024], F32)
    oh_sb = nc.alloc_sbuf_tensor("g_oh", [P, 1024], BF16)
    ol_sb = nc.alloc_sbuf_tensor("g_ol", [P, 1024], BF16)
    o_acc = [o_acc0, o_acc1]

    s2_qg = nc.alloc_semaphore("p2_qg")
    s2_tp = nc.alloc_semaphore("p2_tp")
    s2_qgt = nc.alloc_semaphore("p2_qgt")
    s2_qsp = nc.alloc_semaphore("p2_qsp")
    s2_lg = nc.alloc_semaphore("p2_lg")
    s2_le = nc.alloc_semaphore("p2_le")
    s2_sm = nc.alloc_semaphore("p2_sm")
    s2_at = nc.alloc_semaphore("p2_at")
    s2_ae = nc.alloc_semaphore("p2_ae")
    s2_ld = nc.alloc_semaphore("p2_ld")
    s2_akk = nc.alloc_semaphore("p2_akk")
    s2_osum = nc.alloc_semaphore("p2_osum")
    s2_oacc = nc.alloc_semaphore("p2_oacc")

    from contextlib import ExitStack
    _p2 = ExitStack()
    qg_sb = _p2.enter_context(nc.sbuf_tensor("p2_qg", [1, 2048], F32))
    qgT = _p2.enter_context(nc.sbuf_tensor("p2_qgT", [P, 32], F32))
    qgT_h = _p2.enter_context(nc.sbuf_tensor("p2_qgTh", [P, 32], BF16))
    qgT_l = _p2.enter_context(nc.sbuf_tensor("p2_qgTl", [P, 32], BF16))
    kkTh = _p2.enter_context(nc.sbuf_tensor("p2_kkTh", [P, 4 * 2048], BF16))
    kkTl = _p2.enter_context(nc.sbuf_tensor("p2_kkTl", [P, 4 * 2048], BF16))
    lg_sb = _p2.enter_context(nc.sbuf_tensor("p2_lg", [P, 512], F32))
    lg2 = _p2.enter_context(nc.sbuf_tensor("p2_lg2", [32, 2048], F32))
    e2 = _p2.enter_context(nc.sbuf_tensor("p2_e2", [32, 2048], F32))
    nmax = _p2.enter_context(nc.sbuf_tensor("p2_nmax", [32, 1], F32))
    rsum = _p2.enter_context(nc.sbuf_tensor("p2_rsum", [32, 1], F32))
    rcp2 = _p2.enter_context(nc.sbuf_tensor("p2_rcp", [32, 1], F32))
    alphaT = _p2.enter_context(nc.sbuf_tensor("p2_alphaT", [P, NT * 32], F32))
    kk0 = _p2.enter_context(nc.sbuf_tensor("p2_kk0", [P, 4 * 512], BF16))
    kk1 = _p2.enter_context(nc.sbuf_tensor("p2_kk1", [P, 4 * 512], BF16))
    kkrec = _p2.enter_context(nc.sbuf_tensor("p2_kkrec", [P, 512], F32))
    akk = _p2.enter_context(nc.sbuf_tensor("p2_akk", [P, 2048], F32))
    akk_h0 = _p2.enter_context(nc.sbuf_tensor("p2_akkh0", [P, 2048], BF16))
    akk_l0 = _p2.enter_context(nc.sbuf_tensor("p2_akkl0", [P, 2048], BF16))
    akk_h1 = _p2.enter_context(nc.sbuf_tensor("p2_akkh1", [P, 2048], BF16))
    akk_l1 = _p2.enter_context(nc.sbuf_tensor("p2_akkl1", [P, 2048], BF16))
    akk_hb = [akk_h0, akk_h1]
    akk_lb = [akk_l0, akk_l1]
    kkld = [kk0, kk1]

    with nc.Block() as b:
        @b.sync
        def _(sync):
            for pl, (src_d, dst) in enumerate(((kkh_d, kkTh), (kkl_d, kkTl))):
                for ft in range(4):
                    sync.dma_start_transpose(
                        dst[:, ft * 2048:(ft + 1) * 2048],
                        src_d[:, ft * 128:(ft + 1) * 128]).then_inc(s2_tp, 16)
            sync.wait_ge(s2_qg, 4)
            sync.dma_start(qg_d.rearrange("n dd -> (n dd)").rearrange("(o c) -> o c", o=1),
                           qg_sb[0:1, :]).then_inc(s2_qgt, 16)
            sync.wait_ge(s2_qgt, 16)
            with nc.allow_non_contiguous_dma(reason="tiny qgT build"):
                sync.dma_start(qgT[0:64, :], qg_d.rearrange("n dd -> dd n")).then_inc(s2_qgt, 16)
            sync.wait_ge(s2_qgt, 32)
            sync.dma_start(qgT[64:128, :], qgT[0:64, :]).then_inc(s2_qgt, 16)

        @b.scalar
        def _(sc):
            for ci in range(4):
                sc.activation(qg_sb[0:1, ci * 512:(ci + 1) * 512], pbank[4 + ci][0:1, 0:512],
                              AF.Identity, bias=0.0, scale=1.0).then_inc(s2_qg, 1)

        @b.vector
        def _(v):
            v.memset(o_acc[0][:], 0.0)
            v.wait_ge(s2_qgt, 48)
            v.tensor_copy(qgT_h[:], qgT[:])
            v.drain()
            v.tensor_tensor(qgT_l[:], qgT[:], qgT_h[:], ALU.subtract).then_inc(s2_qsp, 1)
            v.drain()

    with nc.Block() as b:
        @b.tensor
        def _(t):
            t.wait_ge(s2_qsp, 1)
            t.wait_ge(s2_tp, 16 * 8)
            for st in range(NT):
                for g in range(8):
                    seq = st * 8 + g
                    bank, bcol = seq % 4, (seq // 4) * 4
                    po = 64 * (g % 2)
                    co = (g // 2) * 2048 + st * 128
                    mm = None
                    for pas in range(3):
                        la = (kkTh, kkTl, kkTh)[pas]
                        rb = (qgT_h, qgT_h, qgT_l)[pas]
                        mm = t.matmul(pbank[bank][:, bcol:bcol + 4],
                                      la[po:po + 64, co:co + 128],
                                      rb[po:po + 64, 4 * g:4 * g + 4],
                                      start=(pas == 0), stop=(pas == 2))
                    mm.then_inc(s2_lg, 1)

        @b.scalar
        def _(sc):
            sc.wait_ge(s2_lg, NT * 8)
            for bank in range(4):
                dst = lg_sb[:].rearrange("p (sq r) -> p sq r", r=4)[:, bank::4, :]
                sc.activation(dst, pbank[bank][:, 0:128].rearrange("p (sq r) -> p sq r", r=4),
                              AF.Identity, bias=0.0, scale=float(1.0 / S)).then_inc(s2_le, 1)

        @b.tensor
        def _(t2a):
            t2a.wait_ge(s2_le, 4)
            for st in range(NT):
                mm = t2a.transpose(pbank[1 + st // 4][0:32, (st % 4) * 128:(st % 4) * 128 + 128],
                                   lg_sb[:, st * 32:(st + 1) * 32], ident[:])
                mm.then_inc(s2_sm, 1)

        @b.scalar
        def _(sc1):
            for bi in range(4):
                sc1.wait_ge(s2_sm, 4 * (bi + 1))
                sc1.activation(lg2[:, bi * 512:(bi + 1) * 512], pbank[1 + bi][0:32, 0:512],
                               AF.Identity, bias=0.0, scale=1.0).then_inc(s2_le, 1)

        @b.vector
        def _(v):
            v.wait_ge(s2_le, 8)
            v.reduce_max(nmax[:], lg2[:], axis=AX.X, negate=True)
            v.drain()
            v.tensor_copy(junk[0:32, 2:3], nmax[:]).then_inc(s2_sm, 1)

        @b.scalar
        def _(sc2):
            sc2.wait_ge(s2_sm, 17)
            sc2.activation(e2[:], lg2[:], AF.Exp, bias=nmax[:], scale=1.0,
                           accum_out=rsum[:]).then_inc(s2_sm, 1)

        @b.vector
        def _(v2):
            v2.wait_ge(s2_sm, 18)
            v2.reciprocal(rcp2[:], rsum[:])
            v2.drain()
            v2.tensor_scalar(e2[:], e2[:], rcp2[:], float(S), ALU.mult, ALU.mult).then_inc(s2_sm, 1)
            v2.drain()

        @b.tensor
        def _(t2b):
            t2b.wait_ge(s2_sm, 19)
            for st in range(NT):
                if st >= 2:
                    t2b.wait_ge(s2_ae, st - 1)
                mm = t2b.transpose(pbank[5 + st % 2][0:P, 0:32],
                                   e2[:, st * 128:(st + 1) * 128], ident[0:32, 0:32])
                mm.then_inc(s2_at, 1)

        @b.scalar
        def _(sc3):
            for st in range(NT):
                sc3.wait_ge(s2_at, st + 1)
                sc3.activation(alphaT[:, st * 32:(st + 1) * 32], pbank[5 + st % 2][0:P, 0:32],
                               AF.Identity, bias=0.0, scale=1.0).then_inc(s2_ae, 1)

    with nc.Block() as b:
        @b.sync
        def _(sync):
            for st in range(NT):
                if st >= 2:
                    sync.wait_ge(s2_osum, st - 1)
                row0 = st * P
                sync.dma_start(kkld[st % 2][:, 0:512], kkh_d[row0:row0 + P, :]).then_inc(s2_ld, 16)
                sync.dma_start(kkld[st % 2][:, 512:1024], kkl_d[row0:row0 + P, :]).then_inc(s2_ld, 16)
                sync.dma_start(kkld[st % 2][:, 1024:1536], vh_d[row0:row0 + P, :]).then_inc(s2_ld, 16)
                sync.dma_start(kkld[st % 2][:, 1536:2048], vl_d[row0:row0 + P, :]).then_inc(s2_ld, 16)

        @b.vector
        def _(v):
            for st in range(NT):
                v.wait_ge(s2_ld, 64 * (st + 1))
                buf = kkld[st % 2]
                v.tensor_tensor(kkrec[:].bitcast(F32), buf[:, 0:512], buf[:, 512:1024], ALU.add)
                v.drain()
                kk4 = kkrec[:].rearrange("p (g o c) -> p g o c", g=8, o=1).broadcast_to([P, 8, 4, 64])
                al4 = alphaT[:, st * 32:(st + 1) * 32].rearrange(
                    "p (g r o) -> p g r o", g=8, r=4).broadcast_to([P, 8, 4, 64])
                ak4 = akk[:].rearrange("p (g r c) -> p g r c", g=8, r=4)
                v.tensor_tensor(ak4, kk4, al4, ALU.mult)
                v.drain()
                if st >= 2:
                    v.wait_ge(s2_osum, st - 1)
                v.tensor_copy(akk_hb[st % 2][:], akk[:])
                v.drain()
                v.tensor_tensor(akk_lb[st % 2][:], akk[:], akk_hb[st % 2][:],
                                ALU.subtract).then_inc(s2_akk, 1)
                v.drain()
                if st >= 1:
                    v.wait_ge(s2_osum, st)
                    for bank in range(4):
                        dst = o_acc[st % 2][:].rearrange("p (k c) -> p k c", c=64)[:, bank::4, :]
                        sa = o_acc[(st - 1) % 2][:].rearrange("p (k c) -> p k c", c=64)[:, bank::4, :]
                        mmv = v.tensor_tensor(dst, sa,
                                              pbank[bank][:, 0:256].rearrange("p (k c) -> p k c", c=64),
                                              ALU.add)
                    mmv.then_inc(s2_oacc, 1)
                    v.drain()
            v.wait_ge(s2_osum, NT)
            for bank in range(4):
                dst = o_acc[0][:].rearrange("p (k c) -> p k c", c=64)[:, bank::4, :]
                sa = o_acc[(NT - 1) % 2][:].rearrange("p (k c) -> p k c", c=64)[:, bank::4, :]
                mmv = v.tensor_tensor(dst, sa,
                                      pbank[bank][:, 0:256].rearrange("p (k c) -> p k c", c=64),
                                      ALU.add)
            mmv.then_inc(s2_oacc, 1)
            v.drain()
            v.tensor_copy(oh_sb[:], o_acc[0][:])
            v.drain()
            v.tensor_tensor(ol_sb[:], o_acc[0][:], oh_sb[:], ALU.subtract)
            v.drain()

        @b.tensor
        def _(t):
            t.wait_ge(s2_le, 8)
            t.wait_ge(s2_ae, NT)
            for st in range(NT):
                t.wait_ge(s2_akk, st + 1)
                if st >= 1:
                    t.wait_ge(s2_oacc, st)
                v_off = 1024
                buf = kkld[st % 2]
                mm = None
                for gg in range(8):
                    for j in range(2):
                        uu = gg * 2 + j
                        bank = uu % 4
                        col = (uu // 4) * 64
                        for pas in range(3):
                            la = (akk_hb[st % 2], akk_hb[st % 2], akk_lb[st % 2])[pas]
                            rb_off = v_off if pas != 1 else v_off + 512
                            rv = buf[:, rb_off + gg * 64: rb_off + (gg + 1) * 64]
                            mm = t.matmul(pbank[bank][:, col:col + 64],
                                          la[:, (4 * gg + 2 * j) * 64:(4 * gg + 2 * j + 2) * 64],
                                          rv,
                                          start=(pas == 0), stop=(pas == 2))
                mm.then_inc(s2_osum, 1)

    if upto < 3:
        if dbg:
            s_d2 = nc.alloc_semaphore("p2_dbg")
            with nc.Block() as b:
                @b.sync
                def _(sync):
                    sync.dma_start(x1_d[0:P, 0:1024], o_acc[0][:]).then_inc(s_d2, 16)
                    sync.dma_start(x1_d[P:2 * P, 0:NT * 32], alphaT[0:P, :]).then_inc(s_d2, 16)
                    sync.dma_start(x1_d[256:288, 0:2048], lg2[:]).then_inc(s_d2, 16)
                    sync.dma_start(x1_d[288:320, 0:2048], e2[:]).then_inc(s_d2, 16)
                    sync.dma_start(x1_d[320:448, 0:512], lg_sb[:]).then_inc(s_d2, 16)
                    sync.dma_start(x1_d[448:576, 0:32], qgT[:]).then_inc(s_d2, 16)
                    sync.dma_start(xq1_d[0:128, 0:2048], kkTh[:, 0:2048]).then_inc(s_d2, 16)
                    sync.dma_start(x1_d[704:705, 0:2048], qg_sb[:]).then_inc(s_d2, 16)
                    sync.wait_ge(s_d2, 16 * 8)
        finish()
        return

    _p2.close()

    s3_wld = nc.alloc_semaphore("p3_wld")
    s3_wsp = nc.alloc_semaphore("p3_wsp")
    s3_qkt = nc.alloc_semaphore("p3_qkt")
    s3_xp = nc.alloc_semaphore("p3_xp")
    s3_xt = nc.alloc_semaphore("p3_xt")
    s3_rmm = nc.alloc_semaphore("p3_rmm")
    s3_ctx = nc.alloc_semaphore("p3_ctx")
    s3_amm = nc.alloc_semaphore("p3_amm")
    s3_res = nc.alloc_semaphore("p3_res")
    s3_nm = nc.alloc_semaphore("p3_nm")
    s3_sq = nc.alloc_semaphore("p3_sq")
    s3_ln = nc.alloc_semaphore("p3_ln")
    s3_q = nc.alloc_semaphore("p3_q")
    s3_st = nc.alloc_semaphore("p3_st")

    _p3 = ExitStack()
    woh = _p3.enter_context(nc.sbuf_tensor("p3_woh", [P, NT * 2048], BF16))
    wol = _p3.enter_context(nc.sbuf_tensor("p3_wol", [P, NT * 2048], BF16))

    with nc.sbuf_tensor("p3_wst0", [P, 2048], F32) as wst0, \
         nc.sbuf_tensor("p3_wst1", [P, 2048], F32) as wst1:
        wst = [wst0, wst1]
        with nc.Block() as b:
            @b.sync
            def _(sync):
                for a in range(NT):
                    if a >= 2:
                        sync.wait_ge(s3_wsp, a - 1)
                    sync.dma_start(wst[a % 2][:], wo_in[a * P:(a + 1) * P, :]).then_inc(s3_wld, 16)

            @b.vector
            def _(v):
                for a in range(NT):
                    v.wait_ge(s3_wld, 16 * (a + 1))
                    v.tensor_copy(woh[:, a * 2048:(a + 1) * 2048], wst[a % 2][:])
                    v.drain()
                    v.tensor_tensor(wol[:, a * 2048:(a + 1) * 2048], wst[a % 2][:],
                                    woh[:, a * 2048:(a + 1) * 2048], ALU.subtract).then_inc(s3_wsp, 1)
                    v.drain()

    qkth = _p3.enter_context(nc.sbuf_tensor("p3_qkth", [P, NT * 128], BF16))
    qktl = _p3.enter_context(nc.sbuf_tensor("p3_qktl", [P, NT * 128], BF16))
    xphi_sb = _p3.enter_context(nc.sbuf_tensor("p3_xphi", [P, NT * 128], F32))
    xt_sb = _p3.enter_context(nc.sbuf_tensor("p3_xt", [P, 2048], F32))
    ctxth = _p3.enter_context(nc.sbuf_tensor("p3_ctxth", [P, NT * 128], BF16))
    ctxtl = _p3.enter_context(nc.sbuf_tensor("p3_ctxtl", [P, NT * 128], BF16))
    x1b = _p3.enter_context(nc.sbuf_tensor("p3_x1", [P, 2048], F32))
    st0 = _p3.enter_context(nc.sbuf_tensor("p3_st0", [P, 512], F32))
    st1 = _p3.enter_context(nc.sbuf_tensor("p3_st1", [P, 512], F32))
    st2 = _p3.enter_context(nc.sbuf_tensor("p3_st2", [P, 512], F32))
    xq1b = _p3.enter_context(nc.sbuf_tensor("p3_xq1", [P, 2048], BF16))
    sqc = _p3.enter_context(nc.sbuf_tensor("p3_sqc", [P, 8], F32))
    cols = _p3.enter_context(nc.sbuf_tensor("p3_cols", [P, 8], F32))

    ft_order = [j + 4 * i for j in range(4) for i in range(4)]

    with nc.Block() as b:
        @b.sync
        def _(sync):
            for st in range(NT):
                row0 = st * P
                if st >= 1:
                    sync.wait_ge(s3_rmm, st)
                sync.dma_start_transpose(
                    qkth[:].rearrange("p (j r) -> p j r", j=NT),
                    qkh_d[row0:row0 + P, :]).then_inc(s3_qkt, 16)
                sync.dma_start_transpose(
                    qktl[:].rearrange("p (j r) -> p j r", j=NT),
                    qkl_d[row0:row0 + P, :]).then_inc(s3_qkt, 16)
                if st >= 1:
                    sync.wait_ge(s3_ctx, st)
                sync.dma_start(xphi_sb[:].rearrange("p (j r) -> p j r", j=NT),
                               xphiT_d.rearrange("f p s -> p f s")[:, :, row0:row0 + P]).then_inc(s3_xp, 16)
                if st >= 1:
                    sync.wait_ge(s3_res, st)
                sync.dma_start(xt_sb[:], x_in[row0:row0 + P, :]).then_inc(s3_xt, 16)

        @b.tensor
        def _(t):
            for st in range(NT):
                t.wait_ge(s3_qkt, 32 * (st + 1))
                if st >= 1:
                    t.wait_ge(s3_ctx, st)
                mm = None
                for ft in ft_order:
                    for half in range(2):
                        n = 2 * ft + half
                        po = 64 * half
                        ucol = ((n // 4) * 2 + (n % 4) // 2) * 64
                        for pas in range(3):
                            la = (oh_sb, oh_sb, ol_sb)[pas]
                            rb = (qkth, qktl, qkth)[pas]
                            mm = t.matmul(
                                pbank[ft // 4][po:po + 64, (ft % 4) * 128:(ft % 4) * 128 + 128],
                                la[po:po + 64, ucol:ucol + 64],
                                rb[po:po + 64, (n // 2) * 128:(n // 2) * 128 + 128],
                                start=(pas == 0), stop=(pas == 2))
                mm.then_inc(s3_rmm, 1)
                t.wait_ge(s3_ctx, st + 1)
                if st >= 1:
                    t.wait_ge(s3_res, st)
                for oc in range(4):
                    mm = None
                    for pas in range(3):
                        la = (ctxth, ctxth, ctxtl)[pas]
                        rb = (woh, wol, woh)[pas]
                        for kc in range(NT):
                            mm = t.matmul(pbank[4 + oc][:, 0:512],
                                          la[:, kc * 128:(kc + 1) * 128],
                                          rb[:, kc * 2048 + oc * 512: kc * 2048 + (oc + 1) * 512],
                                          start=(pas == 0 and kc == 0),
                                          stop=(pas == 2 and kc == NT - 1))
                    mm.then_inc(s3_amm, 1)

        @b.vector
        def _(v):
            for st in range(NT):
                v.wait_ge(s3_rmm, st + 1)
                v.wait_ge(s3_xp, 16 * (st + 1))
                if st >= 1:
                    v.wait_ge(s3_amm, 4 * st)
                for bk in range(4):
                    v.tensor_tensor(st0[:], pbank[bk][:, 0:512], xphi_sb[:, 512 * bk:512 * (bk + 1)],
                                    ALU.mult)
                    v.drain()
                    v.tensor_copy(ctxth[:, 512 * bk:512 * (bk + 1)], st0[:])
                    v.drain()
                    mmv = v.tensor_tensor(ctxtl[:, 512 * bk:512 * (bk + 1)], st0[:],
                                          ctxth[:, 512 * bk:512 * (bk + 1)], ALU.subtract)
                mmv.then_inc(s3_ctx, 1)
                v.drain()
                v.wait_ge(s3_amm, 4 * (st + 1))
                v.wait_ge(s3_xt, 16 * (st + 1))
                if st >= 1:
                    v.wait_ge(s3_st, 32 * st)
                for oc in range(4):
                    mmv = v.tensor_tensor(x1b[:, oc * 512:(oc + 1) * 512], pbank[4 + oc][:, 0:512],
                                          xt_sb[:, oc * 512:(oc + 1) * 512], ALU.add)
                mmv.then_inc(s3_res, 1)
                v.drain()
                v.reduce_sum(cols[:, 0:1], x1b[:], axis=AX.X)
                v.drain()
                v.tensor_scalar(cols[:, 1:2], cols[:, 0:1], float(-1.0 / D), None, ALU.mult)
                v.drain()
                mmv = v.tensor_scalar(cols[:, 2:3], cols[:, 0:1], float(1.0 / D), None, ALU.mult)
                mmv.then_inc(s3_nm, 1)
                v.drain()
                v.wait_ge(s3_ln, st + 1)
                v.tensor_scalar(x1b[:], x1b[:], cols[:, 2:3], cols[:, 5:6], ALU.subtract, ALU.mult)
                v.drain()
                v.reduce_max(cols[:, 6:7], x1b[:], axis=AX.X, apply_absolute_value=True)
                v.drain()
                v.tensor_scalar(cols[:, 6:7], cols[:, 6:7], float(1e-5), None, ALU.max)
                v.drain()
                v.tensor_scalar(sinv_sb[:, st:st + 1], cols[:, 6:7], float(1.0 / 127.0), None, ALU.mult)
                v.reciprocal(cols[:, 7:8], cols[:, 6:7])
                v.drain()
                v.tensor_scalar(cols[:, 7:8], cols[:, 7:8], 127.0, None, ALU.mult)
                v.drain()
                for sp in range(4):
                    xs = x1b[:, sp * 512:(sp + 1) * 512]
                    v.tensor_scalar(st0[:], xs, cols[:, 7:8], C1, ALU.mult, ALU.add)
                    v.tensor_scalar(st1[:], xs, cols[:, 7:8], C2, ALU.mult, ALU.add)
                    v.drain()
                    v.tensor_scalar(st0[:], st0[:], -C1, None, ALU.add)
                    v.tensor_scalar(st1[:], st1[:], -C2, None, ALU.add)
                    v.drain()
                    v.tensor_sub(st2[:], st1[:], st0[:])
                    v.drain()
                    v.scalar_tensor_tensor(st2[:], st2[:], qsel[:], st0[:], ALU.mult, ALU.add)
                    v.drain()
                    mmv = v.tensor_scalar(xq1b[:, sp * 512:(sp + 1) * 512], st2[:],
                                          -128.0, 127.0, ALU.max, ALU.min)
                mmv.then_inc(s3_q, 1)
                v.drain()

        @b.scalar
        def _(sc):
            for st in range(NT):
                sc.wait_ge(s3_nm, st + 1)
                for sp in range(4):
                    sc.activation(st0[:], x1b[:, sp * 512:(sp + 1) * 512], AF.Square,
                                  bias=cols[:, 1:2], scale=1.0, accum_out=sqc[:, sp:sp + 1])
                sc.drain()
                sc.activation(sqc[:, 4:8], sqc[:, 0:4], AF.Identity, bias=0.0, scale=1.0,
                              accum_out=cols[:, 3:4])
                sc.drain()
                sc.activation(cols[:, 4:5], cols[:, 3:4], AF.Ln, bias=sw_sb[:, 6:7],
                              scale=float(1.0 / D))
                sc.drain()
                sc.activation(cols[:, 5:6], cols[:, 4:5], AF.Exp, bias=0.0,
                              scale=-0.5).then_inc(s3_ln, 1)

        @b.gpsimd
        def _(g):
            for st in range(NT):
                row0 = st * P
                g.wait_ge(s3_q, st + 1)
                g.dma_start(x1_d[row0:row0 + P, :], x1b[:]).then_inc(s3_st, 16)
                g.dma_start(xq1_d[row0:row0 + P, :], xq1b[:]).then_inc(s3_st, 16)
            g.wait_ge(s3_st, 32 * NT)

    _p3.close()

    if upto < 4:
        finish()
        return

    s4p_ld = nc.alloc_semaphore("p4p_ld")
    s4p_rd = nc.alloc_semaphore("p4p_rd")
    s4p_mm = nc.alloc_semaphore("p4p_mm")
    _p4p = ExitStack()
    mst0 = _p4p.enter_context(nc.sbuf_tensor("p4p_m0", [P, 8192], F32))
    mst1 = _p4p.enter_context(nc.sbuf_tensor("p4p_m1", [P, 8192], F32))
    row6 = _p4p.enter_context(nc.sbuf_tensor("p4p_row6", [1, 8], F32))
    mst = [mst0, mst1]
    NWCH = 48

    with nc.Block() as b:
        @b.sync
        def _(sync):
            for i in range(NWCH):
                if i >= 2:
                    sync.wait_ge(s4p_rd, i - 1)
                if i < 32:
                    sync.dma_start(mst[i % 2][:].rearrange("p (a c) -> p a c", c=2048),
                                   wgu_in.rearrange("(a p) c -> p a c", p=P)[:, 4 * i:4 * (i + 1), :]
                                   ).then_inc(s4p_ld, 16)
                else:
                    j = i - 32
                    sync.dma_start(mst[i % 2][:],
                                   wd_in[j * P:(j + 1) * P, :]).then_inc(s4p_ld, 16)

        @b.vector
        def _(v):
            for i in range(NWCH):
                v.wait_ge(s4p_ld, 16 * (i + 1))
                v.reduce_sum(wsum_sb[:, i:i + 1], mst[i % 2][:], axis=AX.X,
                             apply_absolute_value=True).then_inc(s4p_rd, 1)

    if upto < 4.5:
        _p4p.close()
        finish()
        return

    with nc.Block() as b:
        @b.tensor
        def _(t):
            t.matmul(pbank[0][0:1, 0:48], ones_f[:], wsum_sb[:, 0:48],
                     start=True, stop=True).then_inc(s4p_mm, 1)

        @b.vector
        def _(v):
            v.wait_ge(s4p_mm, 1)
            v.reduce_sum(row6[0:1, 0:1], pbank[0][0:1, 0:32], axis=AX.X)
            v.reduce_sum(row6[0:1, 3:4], pbank[0][0:1, 32:48], axis=AX.X)
            v.drain()
            v.tensor_scalar(row6[0:1, 0:1], row6[0:1, 0:1], float(1.0 / (2 * F * D)), None, ALU.mult)
            v.tensor_scalar(row6[0:1, 3:4], row6[0:1, 3:4], float(1.0 / (D * F)), None, ALU.mult)
            v.drain()
            v.tensor_scalar(row6[0:1, 0:1], row6[0:1, 0:1], float(1e-5), None, ALU.max)
            v.tensor_scalar(row6[0:1, 3:4], row6[0:1, 3:4], float(1e-5), None, ALU.max)
            v.drain()
            v.tensor_scalar(row6[0:1, 1:2], row6[0:1, 0:1], 0.5, None, ALU.mult)
            v.tensor_scalar(row6[0:1, 4:5], row6[0:1, 3:4], 0.5, None, ALU.mult)
            v.drain()
            v.tensor_scalar(row6[0:1, 2:3], row6[0:1, 1:2], -1.0, None, ALU.mult)
            v.tensor_scalar(row6[0:1, 5:6], row6[0:1, 4:5], -1.0, None, ALU.mult)
            v.drain()

    with nc.Block() as b:
        @b.tensor
        def _(t):
            t.matmul(pbank[1][0:P, 0:6], ones_row[0:1, :], row6[0:1, 0:6],
                     start=True, stop=True).then_inc(s4p_mm, 1)

        @b.scalar
        def _(sc):
            sc.wait_ge(s4p_mm, 2)
            sc.activation(sw_sb[:, 0:6], pbank[1][0:P, 0:6], AF.Identity, bias=0.0, scale=1.0)
    _p4p.close()

    if upto < 5:
        finish()
        return

    s4_xq = nc.alloc_semaphore("p4a_xq")
    s4_wl = nc.alloc_semaphore("p4a_wl")
    s4_wq = nc.alloc_semaphore("p4a_wq")
    s4_wt = nc.alloc_semaphore("p4a_wt")
    s4_mm = nc.alloc_semaphore("p4a_mm")
    s4_ev = nc.alloc_semaphore("p4a_ev")
    s4_h = nc.alloc_semaphore("p4a_h")
    s4_hst = nc.alloc_semaphore("p4a_hst")

    _p4a = ExitStack()
    xq1T = _p4a.enter_context(nc.sbuf_tensor("p4a_xq1T", [P, NT * 2048], BF16))
    wgT_g0 = _p4a.enter_context(nc.sbuf_tensor("p4a_gT0", [P, NT * 512], BF16))
    wgT_g1 = _p4a.enter_context(nc.sbuf_tensor("p4a_gT1", [P, NT * 512], BF16))
    wgT_u0 = _p4a.enter_context(nc.sbuf_tensor("p4a_uT0", [P, NT * 512], BF16))
    wgT_u1 = _p4a.enter_context(nc.sbuf_tensor("p4a_uT1", [P, NT * 512], BF16))
    wst40 = _p4a.enter_context(nc.sbuf_tensor("p4a_wst0", [P, 2048], F32))
    wst41 = _p4a.enter_context(nc.sbuf_tensor("p4a_wst1", [P, 2048], F32))
    wqp0 = _p4a.enter_context(nc.sbuf_tensor("p4a_wqp0", [P, 2048], BF16))
    wqp1 = _p4a.enter_context(nc.sbuf_tensor("p4a_wqp1", [P, 2048], BF16))
    qa_s = _p4a.enter_context(nc.sbuf_tensor("p4a_qa", [P, 2048], F32))
    sa0 = _p4a.enter_context(nc.sbuf_tensor("p4a_sa0", [P, 512], F32))
    sa1 = _p4a.enter_context(nc.sbuf_tensor("p4a_sa1", [P, 512], F32))
    bb0 = _p4a.enter_context(nc.sbuf_tensor("p4a_bb0", [P, 512], F32))
    bb1 = _p4a.enter_context(nc.sbuf_tensor("p4a_bb1", [P, 512], F32))
    hb0 = _p4a.enter_context(nc.sbuf_tensor("p4a_hb0", [P, 512], F32))
    hb1 = _p4a.enter_context(nc.sbuf_tensor("p4a_hb1", [P, 512], F32))
    wgT_g = [wgT_g0, wgT_g1]
    wgT_u = [wgT_u0, wgT_u1]
    wst4 = [wst40, wst41]
    wqp = [wqp0, wqp1]
    sa_b = [sa0, sa1]
    bb_b = [bb0, bb1]
    hb = [hb0, hb1]

    NFP = 16

    with nc.Block() as b:
        @b.vector
        def _(v):
            v.memset(hmax_sb[:], 0.0)
            v.tensor_scalar(scl_sb[:, 0:NT], sinv_sb[:], sw_sb[:, 0:1], None, ALU.mult)
            v.drain()

        @b.sync
        def _(sync):
            for j in range(NT):
                sync.dma_start_transpose(
                    xq1T[:, j * 2048:(j + 1) * 2048],
                    xq1_d[:, j * P:(j + 1) * P]).then_inc(s4_xq, 16)

    with nc.Block() as b:
        @b.sync
        def _(sync):
            def emit_tp(j):
                fp2, s2 = divmod(j, 8)
                sync.wait_ge(s4_wq, j + 1)
                if fp2 >= 2 and s2 == 0:
                    sync.wait_ge(s4_mm, 32 * (fp2 - 1))
                dstT = (wgT_g if s2 < 4 else wgT_u)[fp2 % 2]
                sl = s2 % 4
                sync.dma_start_transpose(
                    dstT[:].rearrange("p (j c) -> p j c", j=NT)[:, :, sl * 128:(sl + 1) * 128],
                    wqp[j % 2][:]).then_inc(s4_wt, 16)

            for fp in range(NFP):
                for s in range(8):
                    j = fp * 8 + s
                    if j >= 2:
                        sync.wait_ge(s4_wq, j - 1)
                    row0 = (fp * 512 + s * 128) if s < 4 else (F + fp * 512 + (s - 4) * 128)
                    sync.dma_start(wst4[j % 2][:], wgu_in[row0:row0 + P, :]).then_inc(s4_wl, 16)
                    if j >= 2:
                        emit_tp(j - 2)
            emit_tp(NFP * 8 - 2)
            emit_tp(NFP * 8 - 1)

        @b.vector
        def _(v):
            def quants(fp):
                for s in range(8):
                    j = fp * 8 + s
                    v.wait_ge(s4_wl, 16 * (j + 1))
                    if j >= 2:
                        v.wait_ge(s4_wt, 16 * (j - 1))
                    v.tensor_scalar(qa_s[:], wst4[j % 2][:], sw_sb[:, 1:2], None, ALU.is_gt)
                    v.tensor_scalar(wst4[j % 2][:], wst4[j % 2][:], sw_sb[:, 2:3], None, ALU.is_lt)
                    v.drain()
                    v.tensor_tensor(wqp[j % 2][:], qa_s[:], wst4[j % 2][:],
                                    ALU.subtract).then_inc(s4_wq, 1)
                    v.drain()

            def h_units(fp):
                for tt in range(NT):
                    u = fp * NT + tt
                    v.wait_ge(s4_ev, 2 * (u + 1))
                    if u >= 2:
                        v.wait_ge(s4_hst, 16 * (u - 1))
                    v.tensor_tensor(hb[u % 2][:], sa_b[u % 2][:], bb_b[u % 2][:], ALU.mult)
                    v.drain()
                    v.reduce_max(junk[:, 3:4], hb[u % 2][:], axis=AX.X, apply_absolute_value=True)
                    v.drain()
                    v.tensor_tensor(hmax_sb[:, tt:tt + 1], hmax_sb[:, tt:tt + 1], junk[:, 3:4],
                                    ALU.max).then_inc(s4_h, 1)
                    v.drain()

            quants(0)
            quants(1)
            for fp in range(NFP):
                h_units(fp)
                if fp + 2 < NFP:
                    quants(fp + 2)

        @b.tensor
        def _(t):
            t.wait_ge(s4_xq, 16 * NT)
            for fp in range(NFP):
                t.wait_ge(s4_wt, 16 * 8 * (fp + 1))
                for tt in range(NT):
                    u = fp * NT + tt
                    bA = (2 * u) % 4
                    bB = (2 * u + 1) % 4
                    if u >= 2:
                        t.wait_ge(s4_ev, 2 * (u - 1))
                    mm = None
                    for kc in range(NT):
                        mm = t.matmul(pbank[bA][:, 0:512],
                                      xq1T[:, kc * 2048 + tt * P: kc * 2048 + tt * P + P],
                                      wgT_g[fp % 2][:, kc * 512:(kc + 1) * 512],
                                      start=(kc == 0), stop=(kc == NT - 1))
                    mm.then_inc(s4_mm, 1)
                    mm = None
                    for kc in range(NT):
                        mm = t.matmul(pbank[bB][:, 0:512],
                                      xq1T[:, kc * 2048 + tt * P: kc * 2048 + tt * P + P],
                                      wgT_u[fp % 2][:, kc * 512:(kc + 1) * 512],
                                      start=(kc == 0), stop=(kc == NT - 1))
                    mm.then_inc(s4_mm, 1)

        @b.scalar
        def _(sc):
            for fp in range(NFP):
                for tt in range(NT):
                    u = fp * NT + tt
                    bA = (2 * u) % 4
                    bB = (2 * u + 1) % 4
                    sc.wait_ge(s4_mm, 2 * u + 1)
                    if u >= 2:
                        sc.wait_ge(s4_h, u - 1)
                    sc.activation(sa_b[u % 2][:], pbank[bA][:, 0:512], AF.Silu,
                                  bias=0.0, scale=scl_sb[:, tt:tt + 1]).then_inc(s4_ev, 1)
                    sc.wait_ge(s4_mm, 2 * u + 2)
                    sc.activation(bb_b[u % 2][:], pbank[bB][:, 0:512], AF.Identity,
                                  bias=0.0, scale=scl_sb[:, tt:tt + 1]).then_inc(s4_ev, 1)


        @b.gpsimd
        def _(g):
            for fp in range(NFP):
                for tt in range(NT):
                    u = fp * NT + tt
                    g.wait_ge(s4_h, u + 1)
                    g.dma_start(h_d[tt * P:(tt + 1) * P, fp * 512:(fp + 1) * 512],
                                hb[u % 2][:]).then_inc(s4_hst, 16)
            g.wait_ge(s4_hst, 16 * NFP * NT)
    _p4a.close()

    if upto < 5.5:
        finish()
        return

    s42_dl = nc.alloc_semaphore("p42_dl")
    s42_dq = nc.alloc_semaphore("p42_dq")
    s42_dt = nc.alloc_semaphore("p42_dt")
    s42_dst = nc.alloc_semaphore("p42_dst")
    _p42 = ExitStack()
    dst40 = _p42.enter_context(nc.sbuf_tensor("p42_st0", [P, 2048], F32))
    dst41 = _p42.enter_context(nc.sbuf_tensor("p42_st1", [P, 2048], F32))
    dwq0 = _p42.enter_context(nc.sbuf_tensor("p42_wq0", [P, 2048], BF16))
    dwq1 = _p42.enter_context(nc.sbuf_tensor("p42_wq1", [P, 2048], BF16))
    dwt0 = _p42.enter_context(nc.sbuf_tensor("p42_wt0", [P, 2048], BF16))
    dwt1 = _p42.enter_context(nc.sbuf_tensor("p42_wt1", [P, 2048], BF16))
    dqa = _p42.enter_context(nc.sbuf_tensor("p42_qa", [P, 2048], F32))
    dst4 = [dst40, dst41]
    dwq = [dwq0, dwq1]
    dwt = [dwt0, dwt1]

    with nc.Block() as b:
        @b.sync
        def _(sync):
            def emit_dtp(w):
                sync.wait_ge(s42_dq, w + 1)
                if w >= 2:
                    sync.wait_ge(s42_dst, 16 * (w - 1))
                sync.dma_start_transpose(
                    dwt[w % 2][:].rearrange("p (j c) -> p j c", j=NT),
                    dwq[w % 2][:]).then_inc(s42_dt, 16)

            for w in range(64):
                sl, fc = divmod(w, 4)
                if w >= 2:
                    sync.wait_ge(s42_dq, w - 1)
                sync.dma_start(dst4[w % 2][:],
                               wd_in[sl * P:(sl + 1) * P, fc * 2048:(fc + 1) * 2048]
                               ).then_inc(s42_dl, 16)
                if w >= 2:
                    emit_dtp(w - 2)
            emit_dtp(62)
            emit_dtp(63)

        @b.vector
        def _(v):
            for w in range(64):
                v.wait_ge(s42_dl, 16 * (w + 1))
                if w >= 2:
                    v.wait_ge(s42_dt, 16 * (w - 1))
                v.tensor_scalar(dqa[:], dst4[w % 2][:], sw_sb[:, 4:5], None, ALU.is_gt)
                v.tensor_scalar(dst4[w % 2][:], dst4[w % 2][:], sw_sb[:, 5:6], None, ALU.is_lt)
                v.drain()
                v.tensor_tensor(dwq[w % 2][:], dqa[:], dst4[w % 2][:],
                                ALU.subtract).then_inc(s42_dq, 1)
                v.drain()

        @b.gpsimd
        def _(g):
            for w in range(64):
                sl, fc = divmod(w, 4)
                g.wait_ge(s42_dt, 16 * (w + 1))
                g.dma_start(
                    wdqT_d.rearrange("(j p) o -> p j o", p=P)[:, fc * NT:(fc + 1) * NT,
                                                              sl * P:(sl + 1) * P],
                    dwt[w % 2][:].rearrange("p (j c) -> p j c", j=NT)).then_inc(s42_dst, 16)
            g.wait_ge(s42_dst, 16 * 64)
    _p42.close()

    if upto < 6:
        finish()
        return

    s4b_s2 = nc.alloc_semaphore("p4b_s2")
    s4b_hl = nc.alloc_semaphore("p4b_hl")
    s4b_hq = nc.alloc_semaphore("p4b_hq")
    s4b_hs = nc.alloc_semaphore("p4b_hs")
    s4b_ht = nc.alloc_semaphore("p4b_ht")
    s4b_wl = nc.alloc_semaphore("p4b_wl")
    s4b_mm = nc.alloc_semaphore("p4b_mm")
    s4b_ev = nc.alloc_semaphore("p4b_ev")
    s4b_x1 = nc.alloc_semaphore("p4b_x1")
    s4b_nm = nc.alloc_semaphore("p4b_nm")
    s4b_ln = nc.alloc_semaphore("p4b_ln")
    s4b_o = nc.alloc_semaphore("p4b_o")

    hq_d = nc.dram_tensor("s_hq", [S, F], BF16, kind="Internal").ap()

    _p4b = ExitStack()
    _p4h = ExitStack()
    hstg0 = _p4h.enter_context(nc.sbuf_tensor("p4b_h0", [P, 2048], F32))
    hstg1 = _p4h.enter_context(nc.sbuf_tensor("p4b_h1", [P, 2048], F32))
    hqp0 = _p4h.enter_context(nc.sbuf_tensor("p4b_hq0", [P, 2048], BF16))
    hqp1 = _p4h.enter_context(nc.sbuf_tensor("p4b_hq1", [P, 2048], BF16))
    q0 = _p4h.enter_context(nc.sbuf_tensor("p4b_q0", [P, 2048], F32))
    q1 = _p4h.enter_context(nc.sbuf_tensor("p4b_q1", [P, 2048], F32))
    q2 = _p4h.enter_context(nc.sbuf_tensor("p4b_q2", [P, 2048], F32))
    hstg = [hstg0, hstg1]
    hqp = [hqp0, hqp1]

    with nc.Block() as b:
        @b.vector
        def _(v):
            v.tensor_scalar(hmax_sb[:], hmax_sb[:], float(1e-5), None, ALU.max)
            v.drain()
            v.tensor_scalar(scl_sb[:, NT:2 * NT], hmax_sb[:], sw_sb[:, 3:4],
                            float(1.0 / 127.0), ALU.mult, ALU.mult)
            v.reciprocal(sinv_sb[:], hmax_sb[:])
            v.drain()
            v.tensor_scalar(sinv_sb[:], sinv_sb[:], 127.0, None, ALU.mult).then_inc(s4b_s2, 1)
            v.drain()

    with nc.Block() as b:
        @b.sync
        def _(sync):
            for hu in range(64):
                tt, pc = divmod(hu, 4)
                if hu >= 2:
                    sync.wait_ge(s4b_hq, hu - 1)
                sync.dma_start(hstg[hu % 2][:],
                               h_d[tt * P:(tt + 1) * P, pc * 2048:(pc + 1) * 2048]
                               ).then_inc(s4b_hl, 16)

        @b.vector
        def _(v):
            for hu in range(64):
                tt, pc = divmod(hu, 4)
                v.wait_ge(s4b_hl, 16 * (hu + 1))
                if hu >= 2:
                    v.wait_ge(s4b_hs, 16 * (hu - 1))
                scol = sinv_sb[:, tt:tt + 1]
                v.tensor_scalar(q0[:], hstg[hu % 2][:], scol, C1, ALU.mult, ALU.add)
                v.tensor_scalar(q1[:], hstg[hu % 2][:], scol, C2, ALU.mult, ALU.add)
                v.drain()
                v.tensor_scalar(q0[:], q0[:], -C1, None, ALU.add)
                v.tensor_scalar(q1[:], q1[:], -C2, None, ALU.add)
                v.drain()
                v.tensor_sub(q2[:], q1[:], q0[:])
                v.drain()
                v.scalar_tensor_tensor(q2[:], q2[:], qsel[:], q0[:], ALU.mult, ALU.add)
                v.drain()
                v.tensor_scalar(hqp[hu % 2][:], q2[:], -128.0, 127.0,
                                ALU.max, ALU.min).then_inc(s4b_hq, 1)
                v.drain()

        @b.gpsimd
        def _(g):
            for hu in range(64):
                tt, pc = divmod(hu, 4)
                g.wait_ge(s4b_hq, hu + 1)
                g.dma_start(hq_d[tt * P:(tt + 1) * P, pc * 2048:(pc + 1) * 2048],
                            hqp[hu % 2][:]).then_inc(s4b_hs, 16)
            g.wait_ge(s4b_hs, 16 * 64)

    _p4h.close()
    hqT = _p4b.enter_context(nc.sbuf_tensor("p4b_hqT", [P, 64 * 512], BF16))
    wdT0 = _p4b.enter_context(nc.sbuf_tensor("p4b_wdT0", [P, 64 * 256], BF16))
    wdT1 = _p4b.enter_context(nc.sbuf_tensor("p4b_wdT1", [P, 64 * 256], BF16))
    x1b4 = _p4b.enter_context(nc.sbuf_tensor("p4b_x1", [P, 4 * 2048], F32))
    ost = _p4b.enter_context(nc.sbuf_tensor("p4b_ost", [P, 512], F32))
    cols2 = _p4b.enter_context(nc.sbuf_tensor("p4b_cols", [P, 8], F32))
    sqc2 = _p4b.enter_context(nc.sbuf_tensor("p4b_sqc", [P, 8], F32))
    wdT = [wdT0, wdT1]

    with nc.Block() as b:
        @b.sync
        def _(sync):
            for q in range(4):
                if q >= 1:
                    sync.wait_ge(s4b_mm, 32 * q)
                for tl in range(4):
                    tt = q * 4 + tl
                    for pc in range(4):
                        sync.dma_start_transpose(
                            hqT[:].rearrange("p (j c) -> p j c", j=64)[
                                :, pc * NT:(pc + 1) * NT, tl * P:(tl + 1) * P],
                            hq_d[tt * P:(tt + 1) * P, pc * 2048:(pc + 1) * 2048]
                        ).then_inc(s4b_ht, 16)
                for tl in range(4):
                    tt = q * 4 + tl
                    if q >= 1:
                        sync.wait_ge(s4b_o, 64 * q)
                    sync.dma_start(x1b4[:, tl * 2048:(tl + 1) * 2048],
                                   x1_d[tt * P:(tt + 1) * P, :]).then_inc(s4b_x1, 16)
                for oc in range(8):
                    cw = q * 8 + oc
                    if cw >= 2:
                        sync.wait_ge(s4b_ev, 4 * (cw - 1))
                    sync.dma_start(
                        wdT[cw % 2][:].rearrange("p (j c) -> p j c", j=64),
                        wdqT_d.rearrange("(j p) o -> p j o", p=P)[
                            :, :, oc * 256:(oc + 1) * 256]).then_inc(s4b_wl, 16)

        @b.tensor
        def _(t):
            for q in range(4):
                t.wait_ge(s4b_ht, 16 * 16 * (q + 1))
                for oc in range(8):
                    cw = q * 8 + oc
                    t.wait_ge(s4b_wl, 16 * (cw + 1))
                    for tl in range(4):
                        un = cw * 4 + tl
                        bank = un % 4
                        if un >= 4:
                            t.wait_ge(s4b_ev, un - 3)
                        mm = None
                        for kc in range(64):
                            mm = t.matmul(pbank[bank][:, 0:256],
                                          hqT[:, kc * 512 + tl * P: kc * 512 + tl * P + P],
                                          wdT[cw % 2][:, kc * 256:(kc + 1) * 256],
                                          start=(kc == 0), stop=(kc == 63))
                        mm.then_inc(s4b_mm, 1)

        @b.vector
        def _(v):
            for q in range(4):
                for oc in range(8):
                    cw = q * 8 + oc
                    for tl in range(4):
                        un = cw * 4 + tl
                        tt = q * 4 + tl
                        bank = un % 4
                        v.wait_ge(s4b_mm, un + 1)
                        if oc == 0:
                            v.wait_ge(s4b_x1, 16 * (q * 4 + tl + 1))
                        mmv = v.scalar_tensor_tensor(
                            x1b4[:, tl * 2048 + oc * 256: tl * 2048 + (oc + 1) * 256],
                            pbank[bank][:, 0:256], scl_sb[:, NT + tt:NT + tt + 1],
                            x1b4[:, tl * 2048 + oc * 256: tl * 2048 + (oc + 1) * 256],
                            ALU.mult, ALU.add)
                        mmv.then_inc(s4b_ev, 1)
                        v.drain()
                for tl in range(4):
                    tt = q * 4 + tl
                    xsl = x1b4[:, tl * 2048:(tl + 1) * 2048]
                    v.reduce_sum(cols2[:, 0:1], xsl, axis=AX.X)
                    v.drain()
                    v.tensor_scalar(cols2[:, 1:2], cols2[:, 0:1], float(-1.0 / D), None, ALU.mult)
                    v.drain()
                    mmv = v.tensor_scalar(cols2[:, 2:3], cols2[:, 0:1], float(1.0 / D), None, ALU.mult)
                    mmv.then_inc(s4b_nm, 1)
                    v.drain()
                    v.wait_ge(s4b_ln, q * 4 + tl + 1)
                    mmv = v.tensor_scalar(xsl, xsl, cols2[:, 2:3], cols2[:, 5:6],
                                          ALU.subtract, ALU.mult)
                    mmv.then_inc(s4b_nm, 1)
                    v.drain()

        @b.scalar
        def _(sc):
            for q in range(4):
                for tl in range(4):
                    i = q * 4 + tl
                    sc.wait_ge(s4b_nm, 2 * i + 1)
                    xsl = x1b4[:, tl * 2048:(tl + 1) * 2048]
                    for sp in range(4):
                        sc.activation(ost[:], xsl[:, sp * 512:(sp + 1) * 512], AF.Square,
                                      bias=cols2[:, 1:2], scale=1.0, accum_out=sqc2[:, sp:sp + 1])
                    sc.drain()
                    sc.activation(sqc2[:, 4:8], sqc2[:, 0:4], AF.Identity, bias=0.0, scale=1.0,
                                  accum_out=cols2[:, 3:4])
                    sc.drain()
                    sc.activation(cols2[:, 4:5], cols2[:, 3:4], AF.Ln, bias=sw_sb[:, 6:7],
                                  scale=float(1.0 / D))
                    sc.drain()
                    sc.activation(cols2[:, 5:6], cols2[:, 4:5], AF.Exp, bias=0.0,
                                  scale=-0.5).then_inc(s4b_ln, 1)

        @b.gpsimd
        def _(g):
            for q in range(4):
                for tl in range(4):
                    i = q * 4 + tl
                    tt = q * 4 + tl
                    g.wait_ge(s4b_nm, 2 * (i + 1))
                    g.dma_start(out_d[tt * P:(tt + 1) * P, :],
                                x1b4[:, tl * 2048:(tl + 1) * 2048]).then_inc(s4b_o, 16)

    with nc.Block() as b:
        @b.sync
        def _(sync):
            sync.wait_ge(s4b_o, 16 * NT)
    _p4b.close()
    return


def kernel(L0_attn_hidden_states, L0_attn_Wq, L0_attn_Wk, L0_attn_Wv,
           L0_attn_Wphi, L0_attn_Wo, L0_ffn_w_gate_up, L0_ffn_w_down):
    from concourse.bass_utils import run_bass_kernel_spmd

    x_all = np.ascontiguousarray(np.asarray(L0_attn_hidden_states, dtype=np.float32))
    wq = np.ascontiguousarray(np.asarray(L0_attn_Wq, dtype=np.float32))
    wk = np.ascontiguousarray(np.asarray(L0_attn_Wk, dtype=np.float32))
    wv = np.ascontiguousarray(np.asarray(L0_attn_Wv, dtype=np.float32))
    wphi = np.ascontiguousarray(np.asarray(L0_attn_Wphi, dtype=np.float32))
    wo = np.ascontiguousarray(np.asarray(L0_attn_Wo, dtype=np.float32))
    wgu = np.ascontiguousarray(np.asarray(L0_ffn_w_gate_up, dtype=np.float32))
    wdn = np.ascontiguousarray(np.asarray(L0_ffn_w_down, dtype=np.float32))

    nc = bass.Bass("TRN2", target_bir_lowering=False, detect_race_conditions=False)
    build(nc, dbg=False)

    cons = host_constants()
    in_maps = []
    for b in range(8):
        xb = np.ascontiguousarray(x_all[b])
        m = {"x": xb, "xT": np.ascontiguousarray(xb.T),
             "Wq": wq, "Wk": wk, "Wv": wv, "Wphi": wphi, "Wo": wo,
             "w_gu": wgu, "w_dn": wdn}
        m.update(cons)
        in_maps.append(m)

    res = run_bass_kernel_spmd(nc, in_maps, core_ids=list(range(8)))
    out = np.stack([np.asarray(r["out"], dtype=np.float32) for r in res.results], axis=0)
    return out
